# version 02 (final)
# baseline (speedup 1.0000x reference)
import numpy as np
import concourse.bass as bass
import concourse.mybir as mybir
from concourse.bass_utils import run_bass_kernel_spmd

F32 = mybir.dt.float32
BF16 = mybir.dt.bfloat16
AF = mybir.ActivationFunctionType
ALU = mybir.AluOpType
AX = mybir.AxisListType

B, N, D = 2, 8192, 1024
HEADS, DH = 16, 64
T = 2048
NT = 16
FFN_DIM = 4096
NEXP = 8
EPS = 1e-5
GROUPS44 = [[0, 1, 2, 3], [4, 5, 6, 7]]
ENGINES = ("pe", "act", "dve", "pool", "sp")


class Buf:
    __slots__ = ("name", "lw", "rd")

    def __init__(self, name):
        self.name = name
        self.lw = None
        self.rd = []


class Emitter:
    def __init__(self):
        self.ops = {e: [] for e in ENGINES}

    def emit(self, eng, method, *args, r=(), w=(), amt=None, **kwargs):
        if amt is None:
            amt = 16 if method in ("dma_start", "dyndma") else 1
        deps = set()
        for b in r:
            if b.lw is not None:
                deps.add(b.lw)
        for b in w:
            if b.lw is not None:
                deps.add(b.lw)
            deps.update(b.rd)
        idx = len(self.ops[eng])
        tk = (eng, idx)
        self.ops[eng].append(dict(method=method, args=args, kwargs=kwargs,
                                  amt=amt, deps=deps, marked=False))
        for b in r:
            b.rd.append(tk)
        for b in w:
            b.lw = tk
            b.rd = []
        return tk

    def finalize(self, nc, sems, pid_tensor):
        for eng in ENGINES:
            for op in self.ops[eng]:
                kept = set()
                for d in op["deps"]:
                    if d[0] == eng and eng == "pe":
                        continue
                    kept.add(d)
                    self.ops[d[0]][d[1]]["marked"] = True
                op["deps"] = kept
        for eng in ENGINES:
            for op in self.ops[eng]:
                if op["amt"] == 16 or op["method"] in ("dyndma", "collective_compute"):
                    op["marked"] = True
        val_of = {}
        for eng in ENGINES:
            c = 0
            cc = 0
            for i, op in enumerate(self.ops[eng]):
                if op["method"] == "collective_compute":
                    cc += 1
                    val_of[(eng, i)] = ("cc", cc)
                else:
                    if op["marked"]:
                        c += op["amt"]
                    val_of[(eng, i)] = (eng, c)
        for eng in ENGINES:
            hw = {}
            for op in self.ops[eng]:
                waits = {}
                for d in op["deps"]:
                    s, v = val_of[d]
                    if v > waits.get(s, 0):
                        waits[s] = v
                if op["method"] == "dyndma":
                    op["waits"] = sorted(waits.items())
                else:
                    op["waits"] = [(p, v) for p, v in waits.items()
                                   if v > hw.get(p, 0)]
                for p, v in op["waits"]:
                    if v > hw.get(p, 0):
                        hw[p] = v

        def replay(eng):
            def fn(obj):
                sown = sems[eng]
                regs = {}
                if eng == "sp":
                    pid = obj.register("r_pid").__enter__()
                    myq = obj.register("r_myq").__enter__()
                    mq2048 = obj.register("r_mq2048").__enter__()
                    mq524288 = obj.register("r_mq524288").__enter__()
                    tmp = obj.register("r_tmp").__enter__()
                    obj.reg_load(pid, pid_tensor[0:1, 0:1])
                    obj.reg_mod(myq, pid, 4)
                    obj.reg_mul(mq2048, myq, 2048)
                    obj.reg_mul(mq524288, myq, 524288)
                    regs = dict(mq2048=mq2048, mq524288=mq524288, tmp=tmp)
                for op in self.ops[eng]:
                    for p, v in op["waits"]:
                        obj.wait_ge(sems[p], v)
                    if op["method"] == "dyndma":
                        out_ap, tens, const_off, coeff, pairs = op["args"]
                        obj.reg_mov(regs["tmp"], const_off)
                        obj.reg_add(regs["tmp"], regs["tmp"],
                                    regs["mq2048"] if coeff == 2048 else regs["mq524288"])
                        inst = obj.dma_start(out=out_ap,
                                             in_=bass.AP(tens, regs["tmp"], pairs))
                    else:
                        inst = getattr(obj, op["method"])(*op["args"], **op["kwargs"])
                    if op["method"] == "collective_compute":
                        inst.then_inc(sems["cc"], 1)
                    elif op["marked"]:
                        inst.then_inc(sown, op["amt"])
            return fn

        with nc.Block() as block:
            block.gpsimd(replay("pool"))
            block.sync(replay("sp"))
            block.tensor(replay("pe"))
            block.scalar(replay("act"))
            block.vector(replay("dve"))


def build_nc():
    nc = bass.Bass("TRN2", target_bir_lowering=False)

    def din(name, shape):
        return nc.dram_tensor(name, shape, F32, kind="ExternalInput").ap()

    xin = din("xin", [128, NT * 1024])
    wqkv = din("wqkv", [D, 3 * D])
    wo = din("wo", [D, D])
    wq = din("wq", [D, D])
    wkv = din("wkv", [D, D])
    wout_d = din("wout", [D, D])
    wproj = din("wproj", [DH, 1])
    gatewt = din("gatewt", [128, 64])
    f1wt = din("f1wt", [NEXP, D, FFN_DIM])
    f2wt = din("f2wt", [NEXP, FFN_DIM, D])
    bqkv_c = din("bqkv_c", [128, 24])
    bo_c = din("bo_c", [128, 8])
    bout_c = din("bout_c", [128, 8])
    f1b_c = din("f1b_c", [NEXP, 128, 32])
    f2b = din("f2b", [NEXP, D])
    g1rep_d = din("g1rep", [128, 1024])
    b1rep_d = din("b1rep", [128, 1024])
    g2rep_d = din("g2rep", [128, 1024])
    b2rep_d = din("b2rep", [128, 1024])
    lnlg = din("lnlg", [DH, 1])
    lnlb = din("lnlb", [DH, 1])
    lngg = din("lngg", [DH, 1])
    lngb = din("lngb", [DH, 1])
    ident_d = din("ident", [128, 128])
    mask4_d = din("mask4", [128, 4 * 512])
    gmask4_d = din("gmask4", [128, 4 * 512])
    ltri_d = din("ltri", [128, 128])
    perm_d = din("perm", [DH, DH])
    esel_d = din("esel", [8, 1024])
    onesr_d = din("onesr", [1, 128])
    onesc_d = din("onesc", [DH, 1])
    cos_d = din("cos_t", [DH, N])
    sin_d = din("sin_t", [DH, N])

    out = nc.dram_tensor("out", [128, NT * 1024], F32, kind="ExternalOutput").ap()

    park_x = nc.dram_tensor("park_x", [128, NT * 1024], F32, kind="Internal")
    park_r = nc.dram_tensor("park_r", [128, NT * 1024], BF16, kind="Internal")
    ag1_in = nc.dram_tensor("ag1_in", [3 * D, T], BF16, kind="Internal")
    ag1_out = nc.dram_tensor("ag1_out", [4 * 3 * D, T], BF16, kind="Internal")
    ag2_in = nc.dram_tensor("ag2_in", [256, N], BF16, kind="Internal")
    ag2_out = nc.dram_tensor("ag2_out", [1024, N], BF16, kind="Internal")
    ag3_in = nc.dram_tensor("ag3_in", [2 * D, T], BF16, kind="Internal")
    ag3_out = nc.dram_tensor("ag3_out", [4 * 2 * D, T], BF16, kind="Internal")
    myqkv = nc.dram_tensor("myqkv", [3 * D, T], BF16, kind="Internal")
    myoT = nc.dram_tensor("myoT", [D, T], BF16, kind="Internal")

    BASE = 16512
    cur = [BASE]

    def alloc(name, shape, dtype, at=None):
        nbytes = int(np.prod(shape[1:])) * (2 if dtype == BF16 else 4)
        if at is None:
            off = cur[0]
            cur[0] += (nbytes + 31) // 32 * 32
        else:
            off = at
        assert off + nbytes <= 229344, f"SBUF overflow {name} @{off}+{nbytes}"
        return nc.alloc_sbuf_tensor_at(name, list(shape), dtype, offset=off)

    x_f32 = alloc("x_f32", [128, NT * 1024], F32)
    XOFF = BASE
    h_bf = alloc("h_bf", [128, 32768], BF16, at=XOFF)
    qkv3 = alloc("qkv3", [64, 3 * N], BF16, at=XOFF)
    qT_h = alloc("qT_h", [64, N], BF16, at=XOFF)
    kT_h = alloc("kT_h", [64, N], BF16, at=XOFF + 16384)
    vT_h = alloc("vT_h", [64, N], BF16, at=XOFF + 32768)
    vnat = alloc("vnat", [128, 64 * 65], BF16, at=XOFF + 49152)
    qrotT = alloc("qrotT", [64, N], BF16, at=XOFF)
    kvrot65 = alloc("kvrot65", [65, N], BF16, at=XOFF + 16384)
    lkvnT = alloc("lkvnT", [64, N], BF16, at=XOFF + 32768)
    lkv_nat = alloc("lkv_nat", [128, 64 * 65], BF16, at=XOFF + 49152)

    u_bf = alloc("u_bf", [128, NT * 1024], BF16)
    WOFF = XOFF + 65536
    acc_f32 = alloc("acc_f32", [128, 8192], F32, at=WOFF)
    ekv65 = alloc("ekv65", [65, N], BF16, at=WOFF)
    tr1 = alloc("tr1", [65, 4096], BF16, at=WOFF + 16384)
    tr2 = alloc("tr2", [65, 2048], BF16, at=WOFF + 24576)
    tr3 = alloc("tr3", [65, 1024], BF16, at=WOFF + 28672)
    tr4 = alloc("tr4", [65, 512], BF16, at=WOFF + 30720)

    nxT = alloc("nxT", [128, 8 * T], BF16)
    NOFF = WOFF + 32768
    oT_h = alloc("oT_h", [64, N], BF16, at=NOFF)
    oTa = alloc("oTa", [128, 8 * T], BF16, at=NOFF)

    w_arena = alloc("w_arena", [128, 16384], BF16)
    WA = NOFF + 32768
    wslot0 = alloc("wslot0", [128, 8, 128], BF16, at=WA)
    wslot1 = alloc("wslot1", [128, 8, 128], BF16, at=WA + 8192)
    grep = alloc("grep", [128, 1024], F32, at=WA)
    brep = alloc("brep", [128, 1024], F32, at=WA + 8192)
    f1sl0 = alloc("f1sl0", [128, 8, 512], BF16, at=WA)
    f1sl1 = alloc("f1sl1", [128, 8, 512], BF16, at=WA + 8192)
    f2sl0 = alloc("f2sl0", [128, 32, 128], BF16, at=WA + 16384)
    f2sl1 = alloc("f2sl1", [128, 32, 128], BF16, at=WA + 24576)
    nx_bft = alloc("nx_bft", [128, 1024], BF16, at=WA + 16384)
    cosT = alloc("cosT", [64, N], BF16, at=WA)
    sinT = alloc("sinT", [64, N], BF16, at=WA + 16384)

    stage0 = alloc("stage0", [128, T], BF16)
    SOFF = WA + 32768
    stage1 = alloc("stage1", [128, T], BF16)
    xsq = alloc("xsq", [128, 1024], F32, at=SOFF)
    u_f32t = alloc("u_f32t", [128, 1024], F32, at=SOFF + 4096)
    xtile = alloc("xtile", [128, 1024], F32, at=SOFF)
    u1tile = alloc("u1tile", [128, 1024], BF16, at=SOFF + 4096)

    stats = alloc("stats", [128, 16], F32)
    P0 = alloc("P0", [128, 512], BF16)
    P1 = alloc("P1", [128, 512], BF16)
    recf = alloc("recf", [1, 512], F32)
    recb = alloc("recb", [1, 512], BF16)
    o_un = alloc("o_un", [64, 512], BF16)
    glog = alloc("glog", [128, 8], F32)
    gt1 = alloc("gt1", [128, 8], F32)
    gt2 = alloc("gt2", [128, 8], F32)
    gm1 = alloc("gm1", [128, 1], F32)
    gm2 = alloc("gm2", [128, 1], F32)
    ge2 = alloc("ge2", [128, 1], F32)
    gw1 = alloc("gw1", [128, 1], F32)
    gw2 = alloc("gw2", [128, 1], F32)
    gnat_bf = alloc("gnat_bf", [128, 8], BF16)
    gT8 = alloc("gT8", [8, T], BF16)
    gbc_sb = alloc("gbc_sb", [128, 1024], BF16)
    f2b_row = alloc("f2b_row", [1, 1024], BF16)
    f1b_sb = alloc("f1b_sb", [128, 32], F32)
    id_bf = alloc("id_bf", [128, 128], BF16)
    mask4 = alloc("mask4", [128, 4 * 512], BF16)
    gmask4 = alloc("gmask4", [128, 4 * 512], BF16)
    ltri = alloc("ltri", [128, 128], BF16)
    perm = alloc("perm", [DH, DH], BF16)
    esel = alloc("esel", [8, 1024], BF16)
    onesr = alloc("onesr", [1, 128], BF16)
    onesc = alloc("onesc", [DH, 1], BF16)
    bqkv_sb = alloc("bqkv_sb", [128, 24], F32)
    bo_sb = alloc("bo_sb", [128, 8], F32)
    bout_sb = alloc("bout_sb", [128, 8], F32)
    gateW_sb = alloc("gateW_sb", [128, 64], BF16)
    lnlg_sb = alloc("lnlg_sb", [DH, 1], F32)
    lnlb_sb = alloc("lnlb_sb", [DH, 1], F32)
    lngg_sb = alloc("lngg_sb", [DH, 1], F32)
    lngb_sb = alloc("lngb_sb", [DH, 1], F32)
    wproj_sb = alloc("wproj_sb", [DH, 1], BF16)
    rot_a = alloc("rot_a", [64, 512], F32, at=SOFF)
    rot_b = alloc("rot_b", [64, 512], F32, at=SOFF + 4096)
    sr0 = alloc("sr0", [1, 512], F32)
    sr1 = alloc("sr1", [1, 512], F32)
    sr2 = alloc("sr2", [1, 512], F32)
    srb0 = alloc("srb0", [1, 512], BF16)
    srb1 = alloc("srb1", [1, 512], BF16)
    erow_bf = alloc("erow_bf", [1, 512], BF16)
    sq_bf = alloc("sq_bf", [64, 512], BF16)
    gkvT = alloc("gkvT", [64, 512], BF16)
    gkvnT = alloc("gkvnT", [64, 512], BF16)
    gkv_nat = alloc("gkv_nat", [128, 4 * 65], BF16)
    assert cur[0] <= 229344, f"SBUF total overflow: {cur[0]}"

    def mkpsum(name, shape, dtype):
        t = nc.psum_tensor(name, shape, dtype)
        return t.__enter__() if hasattr(t, "__enter__") else t

    pA = mkpsum("pA", [128, 512], F32)
    pB = mkpsum("pB", [128, 512], F32)
    pC = mkpsum("pC", [128, 512], F32)
    pD = mkpsum("pD", [128, 512], F32)
    pE = mkpsum("pE", [128, 512], F32)
    pF = mkpsum("pF", [128, 512], F32)
    pT0 = mkpsum("pT0", [128, 1024], BF16)
    pT1 = mkpsum("pT1", [128, 1024], BF16)

    E = Emitter()
    bufs = {}

    def bof(nm):
        if nm not in bufs:
            bufs[nm] = Buf(nm)
        return bufs[nm]

    bX = bof("X"); bW = bof("W"); bN = bof("NXT")
    bWR0 = bof("WR0"); bWR1 = bof("WR1"); bWR2 = bof("WR2"); bWR3 = bof("WR3")
    bS0 = bof("S0"); bS1 = bof("S1")
    bPA = bof("pA"); bPB = bof("pB"); bPC = bof("pC"); bPD = bof("pD")
    bPE = bof("pE"); bPF = bof("pF"); bPT0 = bof("pT0"); bPT1 = bof("pT1")
    bsm = bof("smalls")
    bP0 = bof("P0b"); bP1 = bof("P1b")
    bK = bof("consts")
    bparkx = bof("park_x"); bparkr = bof("park_r")
    bag1i = bof("ag1i"); bag1o = bof("ag1o")
    bag2i = bof("ag2i"); bag2o = bof("ag2o")
    bag3i = bof("ag3i"); bag3o = bof("ag3o")
    bgbc = bof("gbc"); bgT8 = bof("gT8b")
    bOUT = bof("outb")
    bmyq = bof("myqkv"); bmyo = bof("myoT")

    PBANKS = {id(pA): bPA, id(pB): bPB, id(pC): bPC, id(pD): bPD,
              id(pE): bPE, id(pF): bPF, id(pT0): bPT0, id(pT1): bPT1}

    E.emit("sp", "dma_start", out=x_f32[:, :], in_=xin, w=[bX])
    for dst, src in [(id_bf, ident_d), (mask4, mask4_d), (gmask4, gmask4_d),
                     (ltri, ltri_d), (perm, perm_d), (esel, esel_d),
                     (onesr, onesr_d), (onesc, onesc_d),
                     (gateW_sb, gatewt), (wproj_sb, wproj)]:
        E.emit("pool", "dma_start", out=dst[:, :], in_=src, w=[bK])
    for dst, src in [(bqkv_sb, bqkv_c), (bo_sb, bo_c), (bout_sb, bout_c),
                     (lnlg_sb, lnlg), (lnlb_sb, lnlb), (lngg_sb, lngg),
                     (lngb_sb, lngb)]:
        E.emit("sp", "dma_start", out=dst[:, :], in_=src, w=[bK])
    vnv = vnat.ap().rearrange("p (k c) -> p k c", c=65)
    gnv = gkv_nat.ap().rearrange("p (k c) -> p k c", c=65)
    lnv = lkv_nat.ap().rearrange("p (k c) -> p k c", c=65)
    E.emit("dve", "memset", gnv[:, :, 64], 1.0, w=[bsm])

    def ln_tile(src_ap, src_bufs, out_nx_bft):
        E.emit("dve", "tensor_reduce", stats[:, 0:1], src_ap, AX.X, ALU.add,
               negate=True, r=src_bufs, w=[bsm])
        E.emit("act", "activation", xsq[:, :], src_ap, AF.Square, r=src_bufs, w=[bS0])
        E.emit("dve", "tensor_reduce", stats[:, 1:2], xsq[:, :], AX.X, ALU.add,
               r=[bS0], w=[bsm])
        E.emit("dve", "tensor_scalar", stats[:, 2:3], stats[:, 0:1], 1.0 / 1024, None,
               ALU.mult, r=[bsm], w=[bsm])
        E.emit("dve", "tensor_scalar", stats[:, 3:4], stats[:, 1:2], 1.0 / 1024, None,
               ALU.mult, r=[bsm], w=[bsm])
        E.emit("dve", "tensor_tensor", stats[:, 4:5], stats[:, 2:3], stats[:, 2:3],
               ALU.mult, r=[bsm], w=[bsm])
        E.emit("dve", "tensor_tensor", stats[:, 5:6], stats[:, 3:4], stats[:, 4:5],
               ALU.subtract, r=[bsm], w=[bsm])
        E.emit("dve", "tensor_scalar", stats[:, 5:6], stats[:, 5:6], EPS, None,
               ALU.add, r=[bsm], w=[bsm])
        E.emit("act", "activation", stats[:, 6:7], stats[:, 5:6], AF.Sqrt,
               r=[bsm], w=[bsm])
        E.emit("dve", "reciprocal", stats[:, 7:8], stats[:, 6:7], r=[bsm], w=[bsm])
        E.emit("dve", "tensor_tensor", stats[:, 8:9], stats[:, 2:3], stats[:, 7:8],
               ALU.mult, r=[bsm], w=[bsm])
        if out_nx_bft:
            E.emit("act", "activation", nx_bft[:, :], src_ap, AF.Identity,
                   bias=stats[:, 8:9], scale=stats[:, 7:8],
                   r=src_bufs + [bsm], w=[bWR2])
        else:
            E.emit("act", "activation", u_f32t[:, :], src_ap, AF.Identity,
                   bias=stats[:, 8:9], scale=stats[:, 7:8],
                   r=src_bufs + [bsm], w=[bS1])

    def transpose_nx_to_nxT(t):
        for j in range(8):
            pt = (pT0, pT1)[j % 2]
            bpt = (bPT0, bPT1)[j % 2]
            E.emit("pe", "transpose", pt[:, 0:128], nx_bft[:, 128 * j:128 * (j + 1)],
                   id_bf[:, :], r=[bWR2, bK], w=[bpt])
            E.emit("act", "activation", nxT[:, T * j + 128 * t: T * j + 128 * (t + 1)],
                   pt[:, 0:128], AF.Identity, r=[bpt], w=[bN])

    def ln_phase(src_fn, src_bufs, affine, store_u, g_d, b_d):
        if affine:
            E.emit("sp", "dma_start", out=grep[:, :], in_=g_d, w=[bWR0])
            E.emit("sp", "dma_start", out=brep[:, :], in_=b_d, w=[bWR1])
        for t in range(NT):
            src = src_fn(t)
            ln_tile(src, src_bufs, out_nx_bft=not (affine or store_u))
            if store_u:
                E.emit("dve", "tensor_copy", u_bf[:, 1024 * t:1024 * (t + 1)],
                       u_f32t[:, :], r=[bS1], w=[bW])
            if affine:
                E.emit("dve", "tensor_tensor", xsq[:, :], u_f32t[:, :], grep[:, :],
                       ALU.mult, r=[bS1, bWR0], w=[bS0])
                E.emit("dve", "tensor_tensor", nx_bft[:, :], xsq[:, :], brep[:, :],
                       ALU.add, r=[bS0, bWR1], w=[bWR2])
            elif store_u:
                E.emit("dve", "tensor_copy", nx_bft[:, :], u_f32t[:, :],
                       r=[bS1], w=[bWR2])
            transpose_nx_to_nxT(t)

    def double_ln_phase(g_d, b_d):
        E.emit("sp", "dma_start", out=grep[:, :], in_=g_d, w=[bWR0])
        E.emit("sp", "dma_start", out=brep[:, :], in_=b_d, w=[bWR1])
        for t in range(NT):
            src = x_f32[:, 1024 * t:1024 * (t + 1)]
            ln_tile(src, [bX], out_nx_bft=False)
            E.emit("dve", "tensor_copy", u_bf[:, 1024 * t:1024 * (t + 1)],
                   u_f32t[:, :], r=[bS1], w=[bW])
            E.emit("dve", "tensor_reduce", stats[:, 0:1], u_f32t[:, :], AX.X, ALU.add,
                   negate=True, r=[bS1], w=[bsm])
            E.emit("act", "activation", xsq[:, :], u_f32t[:, :], AF.Square,
                   r=[bS1], w=[bS0])
            E.emit("dve", "tensor_reduce", stats[:, 1:2], xsq[:, :], AX.X, ALU.add,
                   r=[bS0], w=[bsm])
            E.emit("dve", "tensor_scalar", stats[:, 2:3], stats[:, 0:1], 1.0 / 1024,
                   None, ALU.mult, r=[bsm], w=[bsm])
            E.emit("dve", "tensor_scalar", stats[:, 3:4], stats[:, 1:2], 1.0 / 1024,
                   None, ALU.mult, r=[bsm], w=[bsm])
            E.emit("dve", "tensor_tensor", stats[:, 4:5], stats[:, 2:3], stats[:, 2:3],
                   ALU.mult, r=[bsm], w=[bsm])
            E.emit("dve", "tensor_tensor", stats[:, 5:6], stats[:, 3:4], stats[:, 4:5],
                   ALU.subtract, r=[bsm], w=[bsm])
            E.emit("dve", "tensor_scalar", stats[:, 5:6], stats[:, 5:6], EPS, None,
                   ALU.add, r=[bsm], w=[bsm])
            E.emit("act", "activation", stats[:, 6:7], stats[:, 5:6], AF.Sqrt,
                   r=[bsm], w=[bsm])
            E.emit("dve", "reciprocal", stats[:, 7:8], stats[:, 6:7], r=[bsm], w=[bsm])
            E.emit("dve", "tensor_tensor", stats[:, 8:9], stats[:, 2:3], stats[:, 7:8],
                   ALU.mult, r=[bsm], w=[bsm])
            E.emit("act", "activation", xsq[:, :], u_f32t[:, :], AF.Identity,
                   bias=stats[:, 8:9], scale=stats[:, 7:8], r=[bS1, bsm], w=[bS0])
            E.emit("dve", "tensor_tensor", u_f32t[:, :], xsq[:, :], grep[:, :],
                   ALU.mult, r=[bS0, bWR0], w=[bS1])
            E.emit("dve", "tensor_tensor", nx_bft[:, :], u_f32t[:, :], brep[:, :],
                   ALU.add, r=[bS1, bWR1], w=[bWR2])
            transpose_nx_to_nxT(t)

    def proj_to_ag(weights_d, nm, bias_sb, ag_in_t, bag_in, row_off):
        for m in range(nm):
            ws = (wslot0, wslot1)[m % 2]
            bws = (bWR0, bWR1)[m % 2]
            E.emit("pool", "dma_start", out=ws[:, :, :],
                   in_=weights_d[:, 128 * m:128 * (m + 1)].rearrange(
                       "(k p) j -> p k j", p=128), w=[bws])
            st = (stage0, stage1)[m % 2]
            bst = (bS0, bS1)[m % 2]
            for n in range(4):
                pp, bpp = (pA, bPA) if n % 2 == 0 else (pB, bPB)
                for k in range(8):
                    E.emit("pe", "matmul", pp[:, :], ws[:, k, :],
                           nxT[:, T * k + 512 * n: T * k + 512 * (n + 1)],
                           start=(k == 0), stop=(k == 7), r=[bws, bN], w=[bpp])
                if bias_sb is not None:
                    E.emit("act", "activation", st[:, 512 * n:512 * (n + 1)], pp[:, :],
                           AF.Identity, bias=bias_sb[:, m:m + 1], r=[bpp, bK], w=[bst])
                else:
                    E.emit("act", "activation", st[:, 512 * n:512 * (n + 1)], pp[:, :],
                           AF.Identity, r=[bpp], w=[bst])
            E.emit("sp", "dma_start",
                   out=ag_in_t[row_off + 128 * m: row_off + 128 * (m + 1), :],
                   in_=st[:, :], r=[bst], w=[bag_in])

    def attn_normalize(dst_ap, dst_buf):
        E.emit("dve", "reciprocal", recf[0:1, :], pC[64:65, :], r=[bPC], w=[bsm])
        E.emit("act", "activation", recb[0:1, :], recf[0:1, :], AF.Identity,
               r=[bsm], w=[bsm])
        E.emit("pe", "matmul", pD[0:64, :], onesr[:, 0:64], recb[0:1, :],
               start=True, stop=True, r=[bsm, bK], w=[bPD])
        E.emit("act", "activation", o_un[:, :], pC[0:64, :], AF.Identity,
               r=[bPC], w=[bsm])
        E.emit("dve", "tensor_tensor", dst_ap, o_un[:, :], pD[0:64, :], ALU.mult,
               r=[bsm, bPD], w=[dst_buf])

    def causal_attention():
        E.emit("dve", "memset", vnv[:, :, 64], 1.0, w=[bX])
        for s in range(3):
            E.emit("sp", "dyndma", myqkv[1024 * s:1024 * (s + 1), :], ag1_out,
                   1024 * s * T, 524288,
                   [[3072 * T, 4], [T, 256], [1, T]], r=[bag1o], w=[bmyq])
        for h in range(4):
            for rr in range(4):
                for s, dst in ((0, qT_h), (1, kT_h), (2, vT_h)):
                    E.emit("sp", "dma_start", out=dst[:, T * rr:T * (rr + 1)],
                           in_=myqkv[1024 * s + 256 * rr + 64 * h:
                                     1024 * s + 256 * rr + 64 * (h + 1), :],
                           r=[bmyq], w=[bX])
            for kc in range(64):
                pt, bpt = (pT0, bPT0) if kc % 2 == 0 else (pT1, bPT1)
                E.emit("pe", "transpose", pt[:, 0:64], vT_h[:, 128 * kc:128 * (kc + 1)],
                       id_bf[0:64, 0:64], r=[bX, bK], w=[bpt])
                E.emit("act", "activation", vnv[:, kc, 0:64], pt[:, 0:64], AF.Identity,
                       r=[bpt], w=[bX])
            for qc in range(16):
                nkc = 4 * qc + 4
                for kc in range(nkc):
                    pp, bpp = (pA, bPA) if kc % 2 == 0 else (pB, bPB)
                    Pt, bPt = (P0, bP0) if kc % 2 == 0 else (P1, bP1)
                    E.emit("pe", "matmul", pp[:, :], kT_h[:, 128 * kc:128 * (kc + 1)],
                           qT_h[:, 512 * qc:512 * (qc + 1)], start=True, stop=True,
                           r=[bX], w=[bpp])
                    E.emit("act", "activation", Pt[:, :], pp[:, :], AF.Exp, scale=0.125,
                           r=[bpp], w=[bPt])
                    j = kc - 4 * qc
                    if j >= 0:
                        E.emit("dve", "tensor_tensor", Pt[:, :], Pt[:, :],
                               mask4[:, 512 * j:512 * (j + 1)], ALU.mult,
                               r=[bK], w=[bPt])
                    E.emit("pe", "matmul", pC[0:65, :], vnv[:, kc, :], Pt[:, :],
                           start=(kc == 0), stop=(kc == nkc - 1),
                           r=[bX, bPt], w=[bPC])
                attn_normalize(oT_h[:, 512 * qc:512 * (qc + 1)], bN)
            E.emit("sp", "dma_start", out=ag2_in[64 * h:64 * (h + 1), :],
                   in_=oT_h[:, :], r=[bN], w=[bag2i])

    def wo_proj_residual(wmat_d, bias_sb, dst_is_x):
        E.emit("sp", "dyndma", myoT[:, :], ag2_out,
               0, 2048, [[N, 1024], [1, T]], r=[bag2o], w=[bmyo])
        for k in range(8):
            E.emit("sp", "dma_start", out=oTa[:, T * k:T * (k + 1)],
                   in_=myoT[128 * k:128 * (k + 1), :], r=[bmyo], w=[bN])
        for m in range(8):
            ws = (wslot0, wslot1)[m % 2]
            bws = (bWR0, bWR1)[m % 2]
            E.emit("pool", "dma_start", out=ws[:, :, :],
                   in_=wmat_d[:, 128 * m:128 * (m + 1)].rearrange(
                       "(k p) j -> p k j", p=128), w=[bws])
            for n in range(4):
                pp, bpp = (pA, bPA) if n % 2 == 0 else (pB, bPB)
                for k in range(8):
                    E.emit("pe", "matmul", pp[:, :], ws[:, k, :],
                           oTa[:, T * k + 512 * n: T * k + 512 * (n + 1)],
                           start=(k == 0), stop=(k == 7), r=[bws, bN], w=[bpp])
                E.emit("act", "activation", nx_bft[:, 512 * (n % 2):512 * (n % 2) + 512],
                       pp[:, :], AF.Identity, bias=bias_sb[:, m:m + 1],
                       r=[bpp, bK], w=[bWR2])
                for sub in range(4):
                    tt = 4 * n + sub
                    pt, bpt = (pT0, bPT0) if sub % 2 == 0 else (pT1, bPT1)
                    E.emit("pe", "transpose", pt[:, 0:128],
                           nx_bft[:, 512 * (n % 2) + 128 * sub:
                                  512 * (n % 2) + 128 * (sub + 1)],
                           id_bf[:, :], r=[bWR2, bK], w=[bpt])
                    dsl = slice(1024 * tt + 128 * m, 1024 * tt + 128 * (m + 1))
                    if dst_is_x:
                        E.emit("dve", "tensor_tensor", x_f32[:, dsl], x_f32[:, dsl],
                               pt[:, 0:128], ALU.add, r=[bpt, bX], w=[bX])
                    else:
                        E.emit("dve", "tensor_tensor", u_bf[:, dsl], u_bf[:, dsl],
                               pt[:, 0:128], ALU.add, r=[bpt, bW], w=[bW])

    def gates_phase():
        for t in range(NT):
            for k in range(8):
                E.emit("pe", "matmul", pC[:, 0:8],
                       nxT[:, T * k + 128 * t: T * k + 128 * (t + 1)],
                       gateW_sb[:, 8 * k:8 * (k + 1)],
                       start=(k == 0), stop=(k == 7), r=[bN, bK], w=[bPC])
            E.emit("act", "activation", glog[:, :], pC[:, 0:8], AF.Identity,
                   r=[bPC], w=[bsm])
            E.emit("dve", "tensor_reduce", gm1[:, :], glog[:, :], AX.X, ALU.max,
                   r=[bsm], w=[bsm])
            E.emit("dve", "tensor_scalar", gt1[:, :], glog[:, :], gm1[:, :], None,
                   ALU.is_ge, r=[bsm], w=[bsm])
            E.emit("dve", "tensor_scalar", gt2[:, :], gt1[:, :], 1e30, None,
                   ALU.mult, r=[bsm], w=[bsm])
            E.emit("dve", "tensor_tensor", gt2[:, :], glog[:, :], gt2[:, :],
                   ALU.subtract, r=[bsm], w=[bsm])
            E.emit("dve", "tensor_reduce", gm2[:, :], gt2[:, :], AX.X, ALU.max,
                   r=[bsm], w=[bsm])
            E.emit("dve", "tensor_scalar", gt2[:, :], gt2[:, :], gm2[:, :], None,
                   ALU.is_ge, r=[bsm], w=[bsm])
            E.emit("dve", "tensor_tensor", ge2[:, :], gm2[:, :], gm1[:, :],
                   ALU.subtract, r=[bsm], w=[bsm])
            E.emit("act", "activation", ge2[:, :], ge2[:, :], AF.Exp, r=[bsm], w=[bsm])
            E.emit("dve", "tensor_scalar", gw1[:, :], ge2[:, :], 1.0, None,
                   ALU.add, r=[bsm], w=[bsm])
            E.emit("dve", "reciprocal", gw1[:, :], gw1[:, :], r=[bsm], w=[bsm])
            E.emit("dve", "tensor_tensor", gw2[:, :], ge2[:, :], gw1[:, :],
                   ALU.mult, r=[bsm], w=[bsm])
            E.emit("dve", "tensor_scalar", gt1[:, :], gt1[:, :], gw1[:, :], None,
                   ALU.mult, r=[bsm], w=[bsm])
            E.emit("dve", "tensor_scalar", gt2[:, :], gt2[:, :], gw2[:, :], None,
                   ALU.mult, r=[bsm], w=[bsm])
            E.emit("dve", "tensor_tensor", gnat_bf[:, :], gt1[:, :], gt2[:, :],
                   ALU.add, r=[bsm], w=[bsm])
            pt, bpt = (pT0, bPT0) if t % 2 == 0 else (pT1, bPT1)
            E.emit("pe", "transpose", pt[0:8, 0:128], gnat_bf[:, :], id_bf[:, :],
                   r=[bsm, bK], w=[bpt])
            E.emit("act", "activation", gT8[:, 128 * t:128 * (t + 1)], pt[0:8, 0:128],
                   AF.Identity, r=[bpt], w=[bgT8])

    def moe_phase():
        hv = h_bf
        HB = 0
        for tc in range(2):
            E.emit("dve", "memset", acc_f32[:, :], 0.0, w=[bW])
            for e in range(NEXP):
                for half in range(2):
                    E.emit("pe", "matmul", pD[:, :], esel[:, 128 * e:128 * (e + 1)],
                           gT8[:, 1024 * tc + 512 * half:1024 * tc + 512 * (half + 1)],
                           start=True, stop=True, r=[bgT8, bK], w=[bPD])
                    E.emit("act", "activation", gbc_sb[:, 512 * half:512 * (half + 1)],
                           pD[:, :], AF.Identity, r=[bPD], w=[bgbc])
                E.emit("sp", "dma_start", out=f1b_sb[:, :], in_=f1b_c[e, :, :], w=[bsm])
                E.emit("pool", "dma_start", out=f2b_row[:, :], in_=f2b[e:e + 1, :],
                       w=[bsm])
                for s in range(8):
                    sl = (f1sl0, f1sl1)[s % 2]
                    bsl = (bWR0, bWR1)[s % 2]
                    E.emit("pool", "dma_start", out=sl[:, :, :],
                           in_=f1wt[e, :, 512 * s:512 * (s + 1)].rearrange(
                               "(k p) j -> p k j", p=128), w=[bsl])
                    for mm in range(4):
                        m = 4 * s + mm
                        for n in range(2):
                            pp, bpp = (pA, bPA) if (m + n) % 2 == 0 else (pB, bPB)
                            for k in range(8):
                                E.emit("pe", "matmul", pp[:, :],
                                       sl[:, k, 128 * mm:128 * (mm + 1)],
                                       nxT[:, T * k + 1024 * tc + 512 * n:
                                           T * k + 1024 * tc + 512 * (n + 1)],
                                       start=(k == 0), stop=(k == 7),
                                       r=[bsl, bN], w=[bpp])
                            hsl = slice(HB + 1024 * m + 512 * n,
                                        HB + 1024 * m + 512 * (n + 1))
                            E.emit("act", "activation", hv[:, hsl], pp[:, :], AF.Gelu,
                                   bias=f1b_sb[:, m:m + 1], r=[bpp, bsm], w=[bX])
                            E.emit("dve", "tensor_tensor", hv[:, hsl], hv[:, hsl],
                                   gbc_sb[:, 512 * n:512 * (n + 1)], ALU.mult,
                                   r=[bgbc], w=[bX])
                for m2 in range(8):
                    sl = (f2sl0, f2sl1)[m2 % 2]
                    bsl = (bWR2, bWR3)[m2 % 2]
                    E.emit("pool", "dma_start", out=sl[:, :, :],
                           in_=f2wt[e, :, 128 * m2:128 * (m2 + 1)].rearrange(
                               "(k p) j -> p k j", p=128), w=[bsl])
                    for n in range(2):
                        pp, bpp = (pA, bPA) if (m2 + n) % 2 == 0 else (pB, bPB)
                        for k in range(32):
                            E.emit("pe", "matmul", pp[:, :], sl[:, k, :],
                                   hv[:, HB + 1024 * k + 512 * n:
                                       HB + 1024 * k + 512 * (n + 1)],
                                   start=(k == 0), stop=False,
                                   r=[bsl, bX], w=[bpp])
                        E.emit("pe", "matmul", pp[:, :],
                               f2b_row[:, 128 * m2:128 * (m2 + 1)],
                               gbc_sb[0:1, 512 * n:512 * (n + 1)],
                               start=False, stop=True, r=[bsm, bgbc], w=[bpp])
                        asl = slice(1024 * m2 + 512 * n, 1024 * m2 + 512 * (n + 1))
                        E.emit("dve", "tensor_tensor", acc_f32[:, asl],
                               acc_f32[:, asl], pp[:, :], ALU.add,
                               r=[bpp], w=[bW])
            for tt in range(8):
                t = 8 * tc + tt
                E.emit("sp", "dma_start", out=xtile[:, :],
                       in_=park_x[:, 1024 * t:1024 * (t + 1)], r=[bparkx], w=[bS0])
                E.emit("sp", "dma_start", out=u1tile[:, :],
                       in_=park_r[:, 1024 * t:1024 * (t + 1)], r=[bparkr], w=[bS1])
                E.emit("dve", "tensor_scalar", u1tile[:, :], u1tile[:, :], 0.5, None,
                       ALU.mult, r=[], w=[bS1])
                E.emit("dve", "tensor_tensor", xtile[:, :], xtile[:, :], u1tile[:, :],
                       ALU.add, r=[bS1], w=[bS0])
                for m2 in range(8):
                    pt, bpt = (pT0, bPT0) if m2 % 2 == 0 else (pT1, bPT1)
                    E.emit("act", "activation", nx_bft[:, 0:128],
                           acc_f32[:, 1024 * m2 + 128 * tt:1024 * m2 + 128 * (tt + 1)],
                           AF.Identity, r=[bW], w=[bWR2])
                    E.emit("pe", "transpose", pt[:, 0:128], nx_bft[:, 0:128],
                           id_bf[:, :], r=[bWR2, bK], w=[bpt])
                    E.emit("dve", "tensor_tensor", xtile[:, 128 * m2:128 * (m2 + 1)],
                           xtile[:, 128 * m2:128 * (m2 + 1)], pt[:, 0:128], ALU.add,
                           r=[bpt], w=[bS0])
                E.emit("sp", "dma_start", out=park_x[:, 1024 * t:1024 * (t + 1)],
                       in_=xtile[:, :], r=[bS0], w=[bparkx])
        E.emit("sp", "dma_start", out=x_f32[:, :], in_=park_x.ap(), r=[bparkx],
               w=[bX])

    def ffn_call():
        double_ln_phase(g1rep_d, b1rep_d)
        E.emit("sp", "dma_start", out=park_x[:, :], in_=x_f32[:, :], r=[bX],
               w=[bparkx])
        proj_to_ag(wqkv, 24, bqkv_sb, ag1_in, bag1i, 0)
        E.emit("pool", "collective_compute", "AllGather", ALU.bypass,
               replica_groups=GROUPS44, ins=[ag1_in.ap()], outs=[ag1_out.ap()],
               r=[bag1i], w=[bag1o], amt=1)
        causal_attention()
        E.emit("pool", "collective_compute", "AllGather", ALU.bypass,
               replica_groups=GROUPS44, ins=[ag2_in.ap()], outs=[ag2_out.ap()],
               r=[bag2i], w=[bag2o], amt=1)
        wo_proj_residual(wo, bo_sb, dst_is_x=False)
        ln_phase(lambda t: u_bf[:, 1024 * t:1024 * (t + 1)], [bW], True, False,
                 g2rep_d, b2rep_d)
        gates_phase()
        E.emit("sp", "dma_start", out=park_r[:, :], in_=u_bf[:, :], r=[bW],
               w=[bparkr])
        moe_phase()

    def sparse_call():
        ln_phase(lambda t: x_f32[:, 1024 * t:1024 * (t + 1)], [bX], False, False,
                 None, None)
        E.emit("sp", "dma_start", out=park_x[:, :], in_=x_f32[:, :], r=[bX],
               w=[bparkx])
        proj_to_ag(wq, 8, None, ag3_in, bag3i, 0)
        proj_to_ag(wkv, 8, None, ag3_in, bag3i, 1024)
        E.emit("pool", "collective_compute", "AllGather", ALU.bypass,
               replica_groups=GROUPS44, ins=[ag3_in.ap()], outs=[ag3_out.ap()],
               r=[bag3i], w=[bag3o], amt=1)
        E.emit("pool", "dma_start", out=cosT[:, :], in_=cos_d, w=[bWR0, bWR1])
        E.emit("pool", "dma_start", out=sinT[:, :], in_=sin_d, w=[bWR2, bWR3])
        E.emit("dve", "memset", kvrot65[64:65, :], 1.0, w=[bX])
        E.emit("dve", "memset", lnv[:, :, 64], 1.0, w=[bX])
        for s in range(2):
            E.emit("sp", "dyndma", myqkv[1024 * s:1024 * (s + 1), :], ag3_out,
                   1024 * s * T, 524288,
                   [[2048 * T, 4], [T, 256], [1, T]], r=[bag3o], w=[bmyq])
        for h in range(4):
            for rr in range(4):
                E.emit("sp", "dma_start", out=qrotT[:, T * rr:T * (rr + 1)],
                       in_=myqkv[256 * rr + 64 * h:256 * rr + 64 * (h + 1), :],
                       r=[bmyq], w=[bX])
                E.emit("sp", "dma_start",
                       out=kvrot65[0:64, T * rr:T * (rr + 1)],
                       in_=myqkv[1024 + 256 * rr + 64 * h:
                                 1024 + 256 * rr + 64 * (h + 1), :],
                       r=[bmyq], w=[bX])
            for c in range(16):
                cs = slice(512 * c, 512 * (c + 1))
                for tens in (qrotT, kvrot65):
                    E.emit("pe", "matmul", pE[0:64, :], perm[:, :], tens[0:64, cs],
                           start=True, stop=True, r=[bX, bK], w=[bPE])
                    E.emit("dve", "tensor_tensor", rot_a[:, :], tens[0:64, cs],
                           cosT[:, cs], ALU.mult, r=[bX, bWR0, bWR1], w=[bS0])
                    E.emit("dve", "tensor_tensor", rot_b[:, :], pE[0:64, :],
                           sinT[:, cs], ALU.mult, r=[bPE, bWR2, bWR3], w=[bS1])
                    E.emit("dve", "tensor_tensor", tens[0:64, cs], rot_a[:, :],
                           rot_b[:, :], ALU.add, r=[bS0, bS1], w=[bX])
            for c in range(16):
                cs = slice(512 * c, 512 * (c + 1))
                E.emit("dve", "tensor_tensor", sq_bf[:, :], kvrot65[0:64, cs],
                       kvrot65[0:64, cs], ALU.mult, r=[bX], w=[bsm])
                E.emit("pe", "matmul", pE[0:1, :], onesc[:, :], kvrot65[0:64, cs],
                       start=True, stop=True, r=[bX, bK], w=[bPE])
                E.emit("pe", "matmul", pF[0:1, :], onesc[:, :], sq_bf[:, :],
                       start=True, stop=True, r=[bsm, bK], w=[bPF])
                E.emit("dve", "tensor_scalar", sr0[:, :], pE[0:1, :], 1.0 / 64, None,
                       ALU.mult, r=[bPE], w=[bsm])
                E.emit("dve", "tensor_scalar", sr1[:, :], pF[0:1, :], 1.0 / 64, None,
                       ALU.mult, r=[bPF], w=[bsm])
                E.emit("dve", "tensor_tensor", sr2[:, :], sr0[:, :], sr0[:, :],
                       ALU.mult, r=[bsm], w=[bsm])
                E.emit("dve", "tensor_tensor", sr1[:, :], sr1[:, :], sr2[:, :],
                       ALU.subtract, r=[bsm], w=[bsm])
                E.emit("dve", "tensor_scalar", sr1[:, :], sr1[:, :], EPS, None,
                       ALU.add, r=[bsm], w=[bsm])
                E.emit("act", "activation", sr1[:, :], sr1[:, :], AF.Sqrt,
                       r=[bsm], w=[bsm])
                E.emit("dve", "reciprocal", sr1[:, :], sr1[:, :], r=[bsm], w=[bsm])
                E.emit("act", "activation", srb0[:, :], sr1[:, :], AF.Identity,
                       r=[bsm], w=[bsm])
                E.emit("act", "activation", srb1[:, :], sr0[:, :], AF.Identity,
                       r=[bsm], w=[bsm])
                E.emit("pe", "matmul", pD[0:64, :], onesr[:, 0:64], srb1[:, :],
                       start=True, stop=True, r=[bsm, bK], w=[bPD])
                E.emit("dve", "tensor_tensor", sq_bf[:, :], kvrot65[0:64, cs],
                       pD[0:64, :], ALU.subtract, r=[bX, bPD], w=[bsm])
                E.emit("pe", "matmul", pD[0:64, :], onesr[:, 0:64], srb0[:, :],
                       start=True, stop=True, r=[bsm, bK], w=[bPD])
                E.emit("dve", "tensor_tensor", sq_bf[:, :], sq_bf[:, :], pD[0:64, :],
                       ALU.mult, r=[bPD], w=[bsm])
                E.emit("dve", "tensor_scalar", lkvnT[:, cs], sq_bf[:, :],
                       lnlg_sb[:, :], lnlb_sb[:, :], ALU.mult, ALU.add,
                       r=[bsm, bK], w=[bX])
            for kc in range(64):
                pt, bpt = (pT0, bPT0) if kc % 2 == 0 else (pT1, bPT1)
                E.emit("pe", "transpose", pt[:, 0:64], lkvnT[:, 128 * kc:128 * (kc + 1)],
                       id_bf[0:64, 0:64], r=[bX, bK], w=[bpt])
                E.emit("act", "activation", lnv[:, kc, 0:64], pt[:, 0:64], AF.Identity,
                       r=[bpt], w=[bX])
            for c in range(16):
                cs = slice(512 * c, 512 * (c + 1))
                E.emit("pe", "matmul", pE[0:1, :], wproj_sb[:, :], kvrot65[0:64, cs],
                       start=True, stop=True, r=[bX, bK], w=[bPE])
                E.emit("act", "activation", erow_bf[:, :], pE[0:1, :], AF.Exp,
                       r=[bPE], w=[bsm])
                E.emit("pe", "matmul", pD[0:65, :], onesr[:, 0:65], erow_bf[:, :],
                       start=True, stop=True, r=[bsm, bK], w=[bPD])
                E.emit("dve", "tensor_tensor", ekv65[:, cs], kvrot65[:, cs],
                       pD[0:65, :], ALU.mult, r=[bX, bPD], w=[bW])
            e2v = ekv65.ap().rearrange("p (i two) -> p i two", two=2)
            E.emit("dve", "tensor_tensor", tr1[:, :], e2v[:, :, 0], e2v[:, :, 1],
                   ALU.add, r=[bW], w=[bW])
            t2v = tr1.ap().rearrange("p (i two) -> p i two", two=2)
            E.emit("dve", "tensor_tensor", tr2[:, :], t2v[:, :, 0], t2v[:, :, 1],
                   ALU.add, r=[bW], w=[bW])
            t3v = tr2.ap().rearrange("p (i two) -> p i two", two=2)
            E.emit("dve", "tensor_tensor", tr3[:, :], t3v[:, :, 0], t3v[:, :, 1],
                   ALU.add, r=[bW], w=[bW])
            t4v = tr3.ap().rearrange("p (i two) -> p i two", two=2)
            E.emit("dve", "tensor_tensor", tr4[:, :], t4v[:, :, 0], t4v[:, :, 1],
                   ALU.add, r=[bW], w=[bW])
            E.emit("dve", "reciprocal", sr0[:, :], tr4[64:65, :], r=[bW], w=[bsm])
            E.emit("act", "activation", srb0[:, :], sr0[:, :], AF.Identity,
                   r=[bsm], w=[bsm])
            E.emit("pe", "matmul", pD[0:64, :], onesr[:, 0:64], srb0[:, :],
                   start=True, stop=True, r=[bsm, bK], w=[bPD])
            E.emit("dve", "tensor_tensor", gkvT[:, :], tr4[0:64, :], pD[0:64, :],
                   ALU.mult, r=[bW, bPD], w=[bsm])
            E.emit("dve", "tensor_tensor", sq_bf[:, :], gkvT[:, :], gkvT[:, :],
                   ALU.mult, r=[bsm], w=[bsm])
            E.emit("pe", "matmul", pE[0:1, :], onesc[:, :], gkvT[:, :],
                   start=True, stop=True, r=[bsm, bK], w=[bPE])
            E.emit("pe", "matmul", pF[0:1, :], onesc[:, :], sq_bf[:, :],
                   start=True, stop=True, r=[bsm, bK], w=[bPF])
            E.emit("dve", "tensor_scalar", sr0[:, :], pE[0:1, :], 1.0 / 64, None,
                   ALU.mult, r=[bPE], w=[bsm])
            E.emit("dve", "tensor_scalar", sr1[:, :], pF[0:1, :], 1.0 / 64, None,
                   ALU.mult, r=[bPF], w=[bsm])
            E.emit("dve", "tensor_tensor", sr2[:, :], sr0[:, :], sr0[:, :],
                   ALU.mult, r=[bsm], w=[bsm])
            E.emit("dve", "tensor_tensor", sr1[:, :], sr1[:, :], sr2[:, :],
                   ALU.subtract, r=[bsm], w=[bsm])
            E.emit("dve", "tensor_scalar", sr1[:, :], sr1[:, :], EPS, None,
                   ALU.add, r=[bsm], w=[bsm])
            E.emit("act", "activation", sr1[:, :], sr1[:, :], AF.Sqrt,
                   r=[bsm], w=[bsm])
            E.emit("dve", "reciprocal", sr1[:, :], sr1[:, :], r=[bsm], w=[bsm])
            E.emit("act", "activation", srb0[:, :], sr1[:, :], AF.Identity,
                   r=[bsm], w=[bsm])
            E.emit("act", "activation", srb1[:, :], sr0[:, :], AF.Identity,
                   r=[bsm], w=[bsm])
            E.emit("pe", "matmul", pD[0:64, :], onesr[:, 0:64], srb1[:, :],
                   start=True, stop=True, r=[bsm, bK], w=[bPD])
            E.emit("dve", "tensor_tensor", sq_bf[:, :], gkvT[:, :], pD[0:64, :],
                   ALU.subtract, r=[bsm, bPD], w=[bsm])
            E.emit("pe", "matmul", pD[0:64, :], onesr[:, 0:64], srb0[:, :],
                   start=True, stop=True, r=[bsm, bK], w=[bPD])
            E.emit("dve", "tensor_tensor", sq_bf[:, :], sq_bf[:, :], pD[0:64, :],
                   ALU.mult, r=[bPD], w=[bsm])
            E.emit("dve", "tensor_scalar", gkvnT[:, :], sq_bf[:, :],
                   lngg_sb[:, :], lngb_sb[:, :], ALU.mult, ALU.add,
                   r=[bsm, bK], w=[bsm])
            for gc in range(4):
                pt, bpt = (pT0, bPT0) if gc % 2 == 0 else (pT1, bPT1)
                E.emit("pe", "transpose", pt[:, 0:64], gkvnT[:, 128 * gc:128 * (gc + 1)],
                       id_bf[0:64, 0:64], r=[bsm, bK], w=[bpt])
                E.emit("act", "activation", gnv[:, gc, 0:64], pt[:, 0:64], AF.Identity,
                       r=[bpt], w=[bsm])
            for qc in range(16):
                first = True
                for c in range(qc // 4 + 1):
                    pp, bpp = (pA, bPA) if c % 2 == 0 else (pB, bPB)
                    Pt, bPt = (P0, bP0) if c % 2 == 0 else (P1, bP1)
                    E.emit("pe", "matmul", pp[:, :], gkvnT[:, 128 * c:128 * (c + 1)],
                           qrotT[:, 512 * qc:512 * (qc + 1)], start=True, stop=True,
                           r=[bsm, bX], w=[bpp])
                    E.emit("act", "activation", Pt[:, :], pp[:, :], AF.Exp, scale=0.125,
                           r=[bpp], w=[bPt])
                    if c == qc // 4:
                        jj = qc % 4
                        E.emit("dve", "tensor_tensor", Pt[:, :], Pt[:, :],
                               gmask4[:, 512 * jj:512 * (jj + 1)], ALU.mult,
                               r=[bK], w=[bPt])
                    E.emit("pe", "matmul", pC[0:65, :], gnv[:, c, :], Pt[:, :],
                           start=first, stop=False, r=[bsm, bPt], w=[bPC])
                    first = False
                for j in range(4):
                    qw = 4 * qc + j
                    qsl = slice(512 * qc + 128 * j, 512 * qc + 128 * (j + 1))
                    osl = slice(128 * j, 128 * (j + 1))
                    if qw > 0:
                        E.emit("pe", "matmul", pA[:, 0:128],
                               lkvnT[:, 128 * (qw - 1):128 * qw],
                               qrotT[:, qsl], start=True, stop=True, r=[bX], w=[bPA])
                        E.emit("act", "activation", P0[:, 0:128], pA[:, 0:128],
                               AF.Exp, scale=0.125, r=[bPA], w=[bP0])
                        E.emit("pe", "matmul", pC[0:65, osl], lnv[:, qw - 1, :],
                               P0[:, 0:128], start=False, stop=False,
                               r=[bX, bP0], w=[bPC])
                    E.emit("pe", "matmul", pB[:, 0:128],
                           lkvnT[:, 128 * qw:128 * (qw + 1)],
                           qrotT[:, qsl], start=True, stop=True, r=[bX], w=[bPB])
                    E.emit("act", "activation", P1[:, 0:128], pB[:, 0:128],
                           AF.Exp, scale=0.125, r=[bPB], w=[bP1])
                    E.emit("dve", "tensor_tensor", P1[:, 0:128], P1[:, 0:128],
                           ltri[:, :], ALU.mult, r=[bK], w=[bP1])
                    E.emit("pe", "matmul", pC[0:65, osl], lnv[:, qw, :], P1[:, 0:128],
                           start=False, stop=(j == 3), r=[bX, bP1], w=[bPC])
                attn_normalize(oT_h[:, 512 * qc:512 * (qc + 1)], bN)
            E.emit("sp", "dma_start", out=ag2_in[64 * h:64 * (h + 1), :],
                   in_=oT_h[:, :], r=[bN], w=[bag2i])
        E.emit("pool", "collective_compute", "AllGather", ALU.bypass,
               replica_groups=GROUPS44, ins=[ag2_in.ap()], outs=[ag2_out.ap()],
               r=[bag2i], w=[bag2o], amt=1)
        E.emit("sp", "dma_start", out=x_f32[:, :], in_=park_x.ap(), r=[bparkx],
               w=[bX])
        wo_proj_residual(wout_d, bout_sb, dst_is_x=True)

    ffn_call()
    sparse_call()
    ffn_call()
    E.emit("sp", "dma_start", out=out, in_=x_f32[:, :], r=[bX], w=[bOUT])
    E.emit("sp", "nop", r=[bOUT])

    sems = {e: nc.semaphore(f"sem_{e}").__enter__() for e in ENGINES}
    sems["cc"] = nc.semaphore("sem_cc").__enter__()
    E.finalize(nc, sems, nc.partition_id_tensor)
    return nc


_NC_CACHE = []


def _host_inputs(inp):
    x = np.asarray(inp["L0_attn_x"], np.float32)
    c = np.ascontiguousarray

    def relay_x(shard):
        return c(shard.reshape(16, 128, 1024).transpose(1, 0, 2).reshape(128, 16384))

    wqkv = c(np.asarray(inp["L0_ffn_Wqkv"], np.float32))
    wo = c(np.asarray(inp["L0_ffn_Wo"], np.float32))
    wq = c(np.asarray(inp["L0_attn_Wq"], np.float32))
    wkv = c(np.asarray(inp["L0_attn_Wkv"], np.float32))
    wout = c(np.asarray(inp["L0_attn_Wout"], np.float32))
    wproj = c(np.asarray(inp["L0_attn_Wproj"], np.float32).reshape(64, 1))
    gate_W = np.asarray(inp["L0_ffn_gate_W"], np.float32)
    gatewt = c(gate_W.reshape(8, 128, 8).transpose(1, 0, 2).reshape(128, 64))
    f1wt = c(np.asarray(inp["L0_ffn_fc1_w"], np.float32).transpose(0, 2, 1))
    f2wt = c(np.asarray(inp["L0_ffn_fc2_w"], np.float32).transpose(0, 2, 1))
    bqkv = np.asarray(inp["L0_ffn_bqkv"], np.float32)
    bqkv_c = c(bqkv.reshape(24, 128).T)
    bo_c = c(np.asarray(inp["L0_ffn_bo"], np.float32).reshape(8, 128).T)
    bout_c = c(np.asarray(inp["L0_attn_bout"], np.float32).reshape(8, 128).T)
    f1b = np.asarray(inp["L0_ffn_fc1_b"], np.float32)
    f1b_c = c(f1b.reshape(8, 32, 128).transpose(0, 2, 1))
    f2b = c(np.asarray(inp["L0_ffn_fc2_b"], np.float32))
    rep = lambda v: c(np.broadcast_to(np.asarray(v, np.float32)[None, :], (128, 1024)))
    g1rep = rep(inp["L0_ffn_ln1_w"]); b1rep = rep(inp["L0_ffn_ln1_b"])
    g2rep = rep(inp["L0_ffn_ln2_w"]); b2rep = rep(inp["L0_ffn_ln2_b"])
    col = lambda v: c(np.asarray(v, np.float32).reshape(64, 1))
    lnlg = col(inp["L0_attn_ln_local_g"]); lnlb = col(inp["L0_attn_ln_local_b"])
    lngg = col(inp["L0_attn_ln_global_g"]); lngb = col(inp["L0_attn_ln_global_b"])

    ident = np.eye(128, dtype=np.float32)
    kk = np.arange(128)[:, None]
    qq = np.arange(512)[None, :]
    mask4 = np.concatenate([(128 * j + kk <= qq).astype(np.float32)
                            for j in range(4)], axis=1)
    gmask4 = np.concatenate([(qq >= 16 * kk + 16 - 512 * j).astype(np.float32)
                             for j in range(4)], axis=1)
    ltri = (kk <= np.arange(128)[None, :]).astype(np.float32)
    permm = np.zeros((64, 64), np.float32)
    for i in range(32):
        permm[i + 32, i] = -1.0
        permm[i, i + 32] = 1.0
    esel = np.zeros((8, 1024), np.float32)
    for e in range(8):
        esel[e, 128 * e:128 * (e + 1)] = 0.5
    onesr = np.ones((1, 128), np.float32)
    onesc = np.ones((64, 1), np.float32)
    pos = np.arange(N, dtype=np.float64)
    inv = 1.0 / (10000.0 ** (np.arange(0, 64, 2, dtype=np.float64) / 64.0))
    fr = pos[None, :] * inv[:, None]
    emb = np.concatenate([fr, fr], axis=0)
    cos_t = np.cos(emb).astype(np.float32)
    sin_t = np.sin(emb).astype(np.float32)

    common = dict(wqkv=wqkv, wo=wo, wq=wq, wkv=wkv, wout=wout, wproj=wproj,
                  gatewt=gatewt, f1wt=f1wt, f2wt=f2wt, bqkv_c=bqkv_c, bo_c=bo_c,
                  bout_c=bout_c, f1b_c=f1b_c, f2b=f2b, g1rep=g1rep, b1rep=b1rep,
                  g2rep=g2rep, b2rep=b2rep, lnlg=lnlg, lnlb=lnlb, lngg=lngg,
                  lngb=lngb, ident=ident, mask4=mask4, gmask4=gmask4, ltri=ltri,
                  perm=permm, esel=esel, onesr=onesr, onesc=onesc,
                  cos_t=cos_t, sin_t=sin_t)
    in_maps = []
    for cid in range(8):
        b = cid // 4
        q = cid % 4
        m = dict(common)
        m["xin"] = relay_x(x[b, 2048 * q:2048 * (q + 1), :])
        in_maps.append(m)
    return in_maps


def kernel(**inputs):
    if not _NC_CACHE:
        _NC_CACHE.append(build_nc())
    nc = _NC_CACHE[0]
    in_maps = _host_inputs(inputs)
    res = run_bass_kernel_spmd(nc, in_maps, core_ids=list(range(8)))
    xout = np.empty((2, 8192, 1024), np.float32)
    for cid in range(8):
        o = np.asarray(res.results[cid]["out"], np.float32)
        shard = o.reshape(128, 16, 1024).transpose(1, 0, 2).reshape(2048, 1024)
        xout[cid // 4, 2048 * (cid % 4):2048 * (cid % 4 + 1), :] = shard
    return xout
